# version 03
# baseline (speedup 1.0000x reference)
import numpy as np
import concourse.bass as bass
import concourse.mybir as mybir
from concourse.bass_utils import run_bass_kernel_spmd

F32 = mybir.dt.float32
BF16 = mybir.dt.bfloat16
OP = mybir.AluOpType
AF = mybir.ActivationFunctionType
AX = mybir.AxisListType
AP = bass.AP

D = 1024
H = 16
FF = 4096
FI = 2816
NCORES = 8
T = 4096
SLAB = 512
NS = T // SLAB
NCT = SLAB // 128
KC = D // 128
EPS_LN = 1e-5
EPS_Z = 1e-6

ENGS = ("sync", "scalar", "vector", "tensor", "gpsimd")

C_BQ, C_BO, C_B2, C_G1, C_BE1, C_G2, C_B1 = 0, 8, 16, 24, 32, 40, 48
M_SA, M_CA, M_GA, M_SB, M_CB, M_GB, M_CA1, M_CB1 = 0, 8, 16, 24, 32, 40, 48, 56
R_BK, R_BV = 0, D
R_GA, R_GB, R_CB1, R_SB = 0, D, 2 * D, 3 * D

W_COLS = 8192
SL = NCT * D


class Buf:
    __slots__ = ("name", "wr", "rds")

    def __init__(self, name):
        self.name = name
        self.wr = None
        self.rds = []


NPOOL = 12


class CB:

    def __init__(self):
        self.nc = bass.Bass("TRN2", target_bir_lowering=False)
        self.q = {e: [] for e in ENGS}
        self.sem = {e: self.nc.alloc_semaphore(f"S{e}") for e in ENGS}
        for e in ("sync", "gpsimd"):
            for j in range(NPOOL):
                self.sem[f"{e}_d{j}"] = self.nc.alloc_semaphore(f"S{e}d{j}")
        self.sem["cc"] = self.nc.alloc_semaphore("Scc")
        self.cnt = {k: 0 for k in self.sem}
        self.waited = {e: {} for e in ENGS}
        self.pool_i = {"sync": 0, "gpsimd": 0}

    def _hazard_waits(self, eng, reads, writes, extra=()):
        waits = list(extra)
        for b in reads:
            if b.wr is not None:
                waits.append(b.wr)
        for b in writes:
            if b.wr is not None:
                waits.append(b.wr)
            waits.extend(b.rds)
        wmap = {}
        for (key, val) in waits:
            if key == eng:
                continue
            if wmap.get(key, -1) < val:
                wmap[key] = val
        final = []
        for key, val in wmap.items():
            if self.waited[eng].get(key, -1) < val:
                self.waited[eng][key] = val
                final.append((self.sem[key], val))
        return final

    def _finish(self, eng, tok, reads, writes):
        for b in writes:
            b.wr = tok
            b.rds = []
        for b in reads:
            b.rds.append(tok)
        return tok

    def _emit(self, eng, f, reads, writes, inc, drain, semkey=None):
        semkey = semkey or eng
        final = self._hazard_waits(eng, reads, writes)
        sem = self.sem[semkey]

        def g(e, final=final, f=f, inc=inc, drain=drain, sem=sem):
            for s, v in final:
                e.wait_ge(s, v)
            r = f(e)
            if inc:
                r.then_inc(sem, inc)
            if drain:
                e.drain()

        self.q[eng].append(g)
        self.cnt[semkey] += inc
        return self._finish(eng, (semkey, self.cnt[semkey]), reads, writes)

    def _dma(self, eng, f, reads, writes):
        j = self.pool_i[eng]
        self.pool_i[eng] = (j + 1) % NPOOL
        key = f"{eng}_d{j}"
        guard = [(key, self.cnt[key])] if self.cnt[key] else []
        final = self._hazard_waits(eng, reads, writes, extra=guard)
        sem = self.sem[key]

        def g(e, final=final, f=f, sem=sem):
            for s, v in final:
                e.wait_ge(s, v)
            f(e).then_inc(sem, 16)

        self.q[eng].append(g)
        self.cnt[key] += 16
        return self._finish(eng, (key, self.cnt[key]), reads, writes)

    def v(self, f, rd=(), wr=()):
        return self._emit("vector", f, rd, wr, 1, True)

    def a(self, f, rd=(), wr=()):
        return self._emit("scalar", f, rd, wr, 1, True)

    def t(self, f, rd=(), wr=()):
        return self._emit("tensor", f, rd, wr, 1, False)

    def dml(self, f, rd=(), wr=()):
        return self._dma("sync", f, rd, wr)

    def dms(self, f, rd=(), wr=()):
        return self._dma("gpsimd", f, rd, wr)


def ap3(handle, offset, dims):
    return AP(handle, offset, [list(d) for d in dims])


def build_program(debug_names=(), skip_collective=False, bc2_stripes=8, stop_after=None):
    cb = CB()
    nc = cb.nc
    dbg = set(debug_names)

    def dram(name, shape, dt=F32):
        kind = "ExternalOutput" if name in dbg else "Internal"
        return nc.dram_tensor(name, list(shape), dt, kind=kind)

    xin = nc.dram_tensor("xin", [T, D], F32, kind="ExternalInput")
    cfm_in = nc.dram_tensor("cfm", [128, 8], F32, kind="ExternalInput")
    wmod_in = nc.dram_tensor("wmod", [D, 6 * D], F32, kind="ExternalInput")
    wq_in = nc.dram_tensor("wq", [D, D], F32, kind="ExternalInput")
    wk_in = nc.dram_tensor("wk", [D, D], F32, kind="ExternalInput")
    wv_in = nc.dram_tensor("wv", [D, D], F32, kind="ExternalInput")
    wo_in = nc.dram_tensor("wo", [D, D], F32, kind="ExternalInput")
    w1_in = nc.dram_tensor("w1", [D, FF], F32, kind="ExternalInput")
    w2_in = nc.dram_tensor("w2", [FF, D], F32, kind="ExternalInput")
    wg_in = nc.dram_tensor("wg", [FI, D], F32, kind="ExternalInput")
    wu_in = nc.dram_tensor("wu", [FI, D], F32, kind="ExternalInput")
    wd_in = nc.dram_tensor("wd", [D, FI], F32, kind="ExternalInput")
    bq_fm = nc.dram_tensor("bq_fm", [128, 8], F32, kind="ExternalInput")
    bk_row = nc.dram_tensor("bk_row", [1, D], F32, kind="ExternalInput")
    bv_row = nc.dram_tensor("bv_row", [1, D], F32, kind="ExternalInput")
    bo_fm = nc.dram_tensor("bo_fm", [128, 8], F32, kind="ExternalInput")
    b1_fm = nc.dram_tensor("b1_fm", [128, 32], F32, kind="ExternalInput")
    b2_fm = nc.dram_tensor("b2_fm", [128, 8], F32, kind="ExternalInput")
    g1_fm = nc.dram_tensor("g1_fm", [128, 8], F32, kind="ExternalInput")
    be1_fm = nc.dram_tensor("be1_fm", [128, 8], F32, kind="ExternalInput")
    g2_fm = nc.dram_tensor("g2_fm", [128, 8], F32, kind="ExternalInput")
    be2_fm = nc.dram_tensor("be2_fm", [128, 8], F32, kind="ExternalInput")
    ident_in = nc.dram_tensor("ident", [128, 128], F32, kind="ExternalInput")
    out = nc.dram_tensor("out", [T, D], F32, kind="ExternalOutput")

    h_hi_d = dram("h_hi_d", [NS, 128, SL], BF16)
    h_lo_d = dram("h_lo_d", [NS, 128, SL], BF16)
    q_d = dram("q_d", [NS, 128, SL], F32)
    x1_hi_d = dram("x1_hi_d", [NS, 128, SL], BF16)
    x1_lo_d = dram("x1_lo_d", [NS, 128, SL], BF16)
    xq_hi_d = dram("xq_hi_d", [NS, 128, SL], BF16)
    xq_lo_d = dram("xq_lo_d", [NS, 128, SL], BF16)
    ya_d = dram("ya_d", [NS, 128, SL], F32)
    yb_d = dram("yb_d", [NS, 128, SL], F32)
    x2_d = dram("x2_d", [NS, 128, SL], F32)
    y2a_d = dram("y2a_d", [NS, 128, SL], F32)
    y2b_d = dram("y2b_d", [NS, 128, SL], F32)
    garow_d = dram("garow_d", [4, D], F32)
    cc_in = dram("cc_in", [128, 8 * 65], F32)
    cc_out = dram("cc_out", [128, 8 * 65], F32)

    B = {}

    def sb(name, cols, dt=F32, parts=128):
        h = nc.alloc_sbuf_tensor(name, [parts, cols], dt)
        B[name] = Buf(name)
        return h

    ident = sb("ident_s", 128)
    onesr = sb("onesr", 128, parts=1)
    onescol = sb("onescol", 1)
    cfm = sb("cfm_s", 8)
    silc = sb("silc", 8)
    modv = sb("modv", 64)
    sgn8 = sb("sgn8", 16)
    vecs = sb("vecs", 80)
    be2s = sb("be2s", 8)
    r6c = sb("r6c", 2)
    m1b = sb("m1b", 32)
    r1b = sb("r1b", 32)
    m4b = sb("m4b", 4)
    r4b = sb("r4b", 4)
    minis = sb("minis", 32)
    zb = sb("zb", 64)
    kvexp = sb("kvexp", 8 * 130)
    rep = sb("rep", 4 * D)
    xb = sb("xb", SL)
    tb = sb("tb", SL)
    kb = sb("kb", SL)
    qb = sb("qb", SL)
    a2b = sb("a2b", SL)
    x1b = sb("x1b", SL)
    hhi = sb("hhi", SL, BF16)
    hlo = sb("hlo", SL, BF16)
    wa_hi = sb("wa_hi", W_COLS, BF16)
    wa_lo = sb("wa_lo", W_COLS, BF16)
    wb_hi = sb("wb_hi", W_COLS, BF16)
    wb_lo = sb("wb_lo", W_COLS, BF16)
    ub = sb("ub", SLAB)
    t1s = sb("t1s", SLAB)
    t2s = sb("t2s", SLAB)
    hq_hi = sb("hq_hi", 512, BF16)
    hq_lo = sb("hq_lo", 512, BF16)

    pb = [nc.alloc_psum_tensor(f"pbk{i}", [128, 512], F32) for i in range(8)]
    PB = [Buf(f"pb{i}") for i in range(8)]
    Bcc = Buf("cc")
    Bgarow = Buf("garow")
    SP = {k: Buf("sp_" + k) for k in
          ("h", "q", "x1", "xq", "ya", "yb", "x2", "y2a", "y2b", "out")}

    def vcol(i):
        return vecs[:, i:i + 1]

    cb.dml(lambda e: e.dma_start(out=ident[:, :], in_=ident_in.ap()[:, :]),
           wr=[B["ident_s"]])
    cb.dml(lambda e: e.dma_start(out=cfm[:, :], in_=cfm_in.ap()[:, :]),
           wr=[B["cfm_s"]])
    for i, src in enumerate((bq_fm, bo_fm, b2_fm, g1_fm, be1_fm, g2_fm)):
        cb.dml(lambda e, i=i, src=src: e.dma_start(
            out=vecs[:, i * 8:(i + 1) * 8], in_=src.ap()[:, :]),
            wr=[B["vecs"]])
    cb.dml(lambda e: e.dma_start(out=vecs[:, 48:80], in_=b1_fm.ap()[:, :]),
           wr=[B["vecs"]])
    cb.dml(lambda e: e.dma_start(out=be2s[:, :], in_=be2_fm.ap()[:, :]),
           wr=[B["be2s"]])
    cb.v(lambda e: e.memset(r6c[:, 0:1], 6.0), wr=[B["r6c"]])
    cb.v(lambda e: e.reciprocal(r6c[:, 0:1], r6c[:, 0:1]), rd=[B["r6c"]],
         wr=[B["r6c"]])
    cb.v(lambda e: e.memset(r6c[:, 1:2], EPS_LN), wr=[B["r6c"]])
    cb.v(lambda e: e.memset(onesr[:, :], 1.0), wr=[B["onesr"]])
    cb.v(lambda e: e.memset(onescol[:, :], 1.0), wr=[B["onescol"]])

    cb.a(lambda e: e.activation(silc[:, :], cfm[:, :], AF.Silu),
         rd=[B["cfm_s"]], wr=[B["silc"]])
    for g in range(12):
        def ldw(e, g=g):
            o = ap3(xb, 0, [[SL, 128], [512, 8], [1, 512]])
            i = ap3(wmod_in, g * 512,
                    [[6 * D, 128], [128 * 6 * D, 8], [1, 512]])
            return e.dma_start(out=o, in_=i)
        cb.dml(ldw, wr=[B["xb"]])

        def mm(e, g=g):
            r = None
            for mc in range(4):
                for kc in range(KC):
                    r = e.matmul(
                        pb[0][:, g * 4 + mc:g * 4 + mc + 1],
                        xb[:, kc * 512 + mc * 128:kc * 512 + (mc + 1) * 128],
                        silc[:, kc:kc + 1],
                        start=(kc == 0), stop=(kc == KC - 1))
            return r
        cb.t(mm, rd=[B["xb"], B["silc"]], wr=[PB[0]])
    cb.v(lambda e: e.tensor_copy(modv[:, 0:48], pb[0][:, 0:48]),
         rd=[PB[0]], wr=[B["modv"]])
    cb.v(lambda e: e.tensor_scalar_add(modv[:, M_CA1:M_CA1 + 8],
                                       modv[:, M_CA:M_CA + 8], 1.0),
         rd=[B["modv"]], wr=[B["modv"]])
    cb.v(lambda e: e.tensor_scalar_add(modv[:, M_CB1:M_CB1 + 8],
                                       modv[:, M_CB:M_CB + 8], 1.0),
         rd=[B["modv"]], wr=[B["modv"]])
    for j, col in enumerate((M_GA, M_GB)):
        cb.v(lambda e, j=j, col=col: e.tensor_scalar(
            sgn8[:, j * 8:(j + 1) * 8], modv[:, col:col + 8], 0.0, 2.0,
            OP.is_ge, OP.mult), rd=[B["modv"]], wr=[B["sgn8"]])
        cb.v(lambda e, j=j: e.tensor_scalar_sub(
            sgn8[:, j * 8:(j + 1) * 8], sgn8[:, j * 8:(j + 1) * 8], 1.0),
            rd=[B["sgn8"]], wr=[B["sgn8"]])
        cb.a(lambda e, col=col: e.activation(
            modv[:, col:col + 8], modv[:, col:col + 8], AF.Abs),
            rd=[B["modv"]], wr=[B["modv"]])
        cb.v(lambda e, col=col: e.tensor_scalar_max(
            modv[:, col:col + 8], modv[:, col:col + 8], 0.25),
            rd=[B["modv"]], wr=[B["modv"]])
        cb.v(lambda e, j=j, col=col: e.tensor_mul(
            modv[:, col:col + 8], modv[:, col:col + 8],
            sgn8[:, j * 8:(j + 1) * 8]),
            rd=[B["modv"], B["sgn8"]], wr=[B["modv"]])

    for j, col in enumerate((M_GA, M_GB, M_CB1, M_SB)):
        def st(e, j=j, col=col):
            o = ap3(garow_d, j * D, [[1, 128], [128, 8]])
            return e.dma_start(out=o, in_=modv[:, col:col + 8],
                               allow_slow_non_contiguous=True)
        cb.dms(st, rd=[B["modv"]], wr=[Bgarow])

    def replicate(dst_col, src_dram, src_off):
        def f(e):
            o = ap3(rep, dst_col, [[4 * D, 128], [1, D]])
            i = ap3(src_dram, src_off, [[0, 128], [1, D]])
            return e.dma_start(out=o, in_=i)
        cb.dml(f, rd=[Bgarow], wr=[B["rep"]])

    replicate(R_BK, bk_row, 0)
    replicate(R_BV, bv_row, 0)

    def load_split(w_dram, wcols, row0, col0, nrows, ncols, hi, lo, hib, lob,
                   dcol0=0):
        nkc = nrows // 128

        def cast(e):
            o = ap3(hi, dcol0, [[W_COLS, 128], [ncols, nkc], [1, ncols]])
            i = ap3(w_dram, row0 * wcols + col0,
                    [[wcols, 128], [128 * wcols, nkc], [1, ncols]])
            return e.dma_start(out=o, in_=i)
        cb.dms(cast, wr=[hib])
        hkc = max(1, nkc // 2)
        nhalf = (nkc + hkc - 1) // hkc
        for hh in range(nhalf):
            def full(e, hh=hh):
                o = ap3(xb, 0, [[SL, 128], [ncols, hkc], [1, ncols]])
                i = ap3(w_dram, (row0 + hh * hkc * 128) * wcols + col0,
                        [[wcols, 128], [128 * wcols, hkc], [1, ncols]])
                return e.dma_start(out=o, in_=i)
            cb.dml(full, wr=[B["xb"]])
            cb.v(lambda e, hh=hh: e.tensor_sub(
                ap3(lo, dcol0 + hh * hkc * ncols,
                    [[W_COLS, 128], [1, hkc * ncols]]),
                ap3(xb, 0, [[SL, 128], [1, hkc * ncols]]),
                ap3(hi, dcol0 + hh * hkc * ncols,
                    [[W_COLS, 128], [1, hkc * ncols]])),
                rd=[B["xb"], hib], wr=[lob])

    load_split(wq_in, D, 0, 0, D, 1024, wa_hi, wa_lo, B["wa_hi"], B["wa_lo"])

    cb.v(lambda e: e.memset(qb[:, 0:520], 0.0), wr=[B["qb"]])
    cb.v(lambda e: e.memset(kvexp[:, :], 0.0), wr=[B["kvexp"]])

    def ln_stats_tm(src, src_b, sqout, sqout_b, mdst, mdst_b, rdst, rdst_b,
                    col0, eps):
        for c in range(NCT):
            cb.v(lambda e, c=c: e.tensor_reduce(
                minis[:, c:c + 1], src[:, c * D:(c + 1) * D], AX.X, OP.add),
                rd=[src_b], wr=[B["minis"]])
            cb.a(lambda e, c=c: e.activation(
                sqout[:, c * D:(c + 1) * D], src[:, c * D:(c + 1) * D],
                AF.Square, accum_out=minis[:, 4 + c:5 + c]),
                rd=[src_b], wr=[sqout_b, B["minis"]])
        cb.v(lambda e: e.tensor_scalar_mul(
            mdst[:, col0:col0 + NCT], minis[:, 0:NCT], 1.0 / D),
            rd=[B["minis"]], wr=[mdst_b])
        cb.v(lambda e: e.tensor_mul(
            minis[:, 8:8 + NCT], mdst[:, col0:col0 + NCT],
            mdst[:, col0:col0 + NCT]), rd=[mdst_b], wr=[B["minis"]])
        cb.v(lambda e: e.scalar_tensor_tensor(
            minis[:, 12:12 + NCT], minis[:, 4:4 + NCT], 1.0 / D,
            minis[:, 8:8 + NCT], OP.mult, OP.subtract),
            rd=[B["minis"]], wr=[B["minis"]])
        cb.a(lambda e: e.activation(
            minis[:, 16:16 + NCT], minis[:, 12:12 + NCT], AF.Sqrt,
            bias=r6c[:, 1:2]),
            rd=[B["minis"], B["r6c"]], wr=[B["minis"]])
        cb.v(lambda e: e.reciprocal(
            rdst[:, col0:col0 + NCT], minis[:, 16:16 + NCT]),
            rd=[B["minis"]], wr=[rdst_b])

    def ln_apply_tm(src, src_b, dst, dst_b, mdst, mdst_b, rdst, rdst_b, col0,
                    rw):
        for c in range(NCT):
            def f(e, c=c):
                r_b = ap3(rdst, col0 + c, [[rw, 128], [0, D]])
                return e.scalar_tensor_tensor(
                    dst[:, c * D:(c + 1) * D], src[:, c * D:(c + 1) * D],
                    mdst[:, col0 + c:col0 + c + 1], r_b, OP.subtract, OP.mult)
            cb.v(f, rd=[src_b, mdst_b, rdst_b], wr=[dst_b])

    def tr_tm_to_fm(src, src_b, dsthi, dsthi_b, dstlo, dstlo_b, banks,
                    affine=None, aff_tmp=None, aff_tmp_b=None, dcol0=0):
        for f in range(KC):
            bk = banks[f % 2]
            pbk = pb[bk]

            def tr(e, f=f, pbk=pbk):
                r = None
                for c in range(NCT):
                    r = e.transpose(
                        pbk[:, c * 128:(c + 1) * 128],
                        src[:, c * D + f * 128:c * D + (f + 1) * 128],
                        ident[:, :])
                return r
            cb.t(tr, rd=[src_b, B["ident_s"]], wr=[PB[bk]])
            if affine is not None:
                acol, scol = affine

                def ev(e, f=f, pbk=pbk, acol=acol, scol=scol):
                    sa_b = ap3(modv, scol + f, [[64, 128], [0, SLAB]])
                    return e.scalar_tensor_tensor(
                        aff_tmp[:, f * SLAB:(f + 1) * SLAB], pbk[:, 0:SLAB],
                        modv[:, acol + f:acol + f + 1], sa_b, OP.mult, OP.add)
                cb.v(ev, rd=[PB[bk], B["modv"]], wr=[aff_tmp_b])
                cb.a(lambda e, f=f: e.activation(
                    dsthi[:, dcol0 + f * SLAB:dcol0 + (f + 1) * SLAB],
                    aff_tmp[:, f * SLAB:(f + 1) * SLAB], AF.Identity),
                    rd=[aff_tmp_b], wr=[dsthi_b])
                cb.v(lambda e, f=f: e.tensor_sub(
                    dstlo[:, dcol0 + f * SLAB:dcol0 + (f + 1) * SLAB],
                    aff_tmp[:, f * SLAB:(f + 1) * SLAB],
                    dsthi[:, dcol0 + f * SLAB:dcol0 + (f + 1) * SLAB]),
                    rd=[aff_tmp_b, dsthi_b], wr=[dstlo_b])
            else:
                cb.a(lambda e, f=f, pbk=pbk: e.activation(
                    dsthi[:, dcol0 + f * SLAB:dcol0 + (f + 1) * SLAB],
                    pbk[:, 0:SLAB],
                    AF.Identity), rd=[PB[bk]], wr=[dsthi_b])
                cb.v(lambda e, f=f, pbk=pbk: e.tensor_sub(
                    dstlo[:, dcol0 + f * SLAB:dcol0 + (f + 1) * SLAB],
                    pbk[:, 0:SLAB],
                    dsthi[:, dcol0 + f * SLAB:dcol0 + (f + 1) * SLAB]),
                    rd=[PB[bk], dsthi_b], wr=[dstlo_b])

    def mm3_fm(bank, lhs_hi, lhs_lo, lhsb, rhs_hi, rhs_lo, rhsb, fcol, wcol0,
               wcols, nkc=KC, rcol0=0, rstep=SLAB):
        pbk = pb[bank]

        def mm(e):
            r = None
            n = 0
            for (wsrc, xsrc) in ((lhs_hi, rhs_hi), (lhs_hi, rhs_lo),
                                 (lhs_lo, rhs_hi)):
                for kc in range(nkc):
                    r = e.matmul(
                        pbk[:, 0:SLAB],
                        wsrc[:, wcol0 + kc * wcols + fcol * 128:
                             wcol0 + kc * wcols + (fcol + 1) * 128],
                        xsrc[:, rcol0 + kc * rstep:rcol0 + kc * rstep + SLAB],
                        start=(n == 0), stop=(n == 3 * nkc - 1))
                    n += 1
            return r
        return cb.t(mm, rd=[lhsb[0], lhsb[1], rhsb[0], rhsb[1]],
                    wr=[PB[bank]])

    def fm_ln(srcbuf, src_b, gcol, bvec, bvec_b, bvw, bcol0, dst, dst_b, eps,
              tmp, tmp_b):
        def f1(e):
            r = None
            for kc in range(KC):
                r = e.matmul(pb[7][0:1, 0:SLAB], onescol[:, :],
                             srcbuf[:, kc * SLAB:(kc + 1) * SLAB],
                             start=(kc == 0), stop=(kc == KC - 1))
            return r
        cb.t(f1, rd=[src_b, B["onescol"]], wr=[PB[7]])
        cb.a(lambda e: e.activation(tmp[:, :], srcbuf[:, :], AF.Square),
             rd=[src_b], wr=[tmp_b])

        def f2(e):
            r = None
            for kc in range(KC):
                r = e.matmul(pb[6][0:1, 0:SLAB], onescol[:, :],
                             tmp[:, kc * SLAB:(kc + 1) * SLAB],
                             start=(kc == 0), stop=(kc == KC - 1))
            return r
        cb.t(f2, rd=[tmp_b, B["onescol"]], wr=[PB[6]])
        cb.v(lambda e: e.tensor_scalar_mul(ub[0:1, 0:SLAB],
                                           pb[7][0:1, 0:SLAB], 1.0 / D),
             rd=[PB[7]], wr=[B["ub"]])
        cb.v(lambda e: e.tensor_mul(t1s[0:1, 0:SLAB], ub[0:1, 0:SLAB],
                                    ub[0:1, 0:SLAB]),
             rd=[B["ub"]], wr=[B["t1s"]])
        cb.v(lambda e: e.scalar_tensor_tensor(
            t2s[0:1, 0:SLAB], pb[6][0:1, 0:SLAB], 1.0 / D, t1s[0:1, 0:SLAB],
            OP.mult, OP.subtract), rd=[PB[6], B["t1s"]], wr=[B["t2s"]])
        cb.a(lambda e: e.activation(t1s[0:1, 0:SLAB], t2s[0:1, 0:SLAB],
                                    AF.Sqrt, bias=r6c[0:1, 1:2]),
             rd=[B["t2s"], B["r6c"]], wr=[B["t1s"]])
        cb.v(lambda e: e.reciprocal(t2s[0:1, 0:SLAB], t1s[0:1, 0:SLAB]),
             rd=[B["t1s"]], wr=[B["t2s"]])
        cb.v(lambda e: e.tensor_mul(t1s[0:1, 0:SLAB], ub[0:1, 0:SLAB],
                                    t2s[0:1, 0:SLAB]),
             rd=[B["ub"], B["t2s"]], wr=[B["t1s"]])
        cb.t(lambda e: e.matmul(pb[7][:, 0:SLAB], onesr[:, :],
                                t2s[0:1, 0:SLAB], start=True, stop=True),
             rd=[B["t2s"], B["onesr"]], wr=[PB[7]])
        cb.t(lambda e: e.matmul(pb[6][:, 0:SLAB], onesr[:, :],
                                t1s[0:1, 0:SLAB], start=True, stop=True),
             rd=[B["t1s"], B["onesr"]], wr=[PB[6]])
        for f in range(KC):
            cb.v(lambda e, f=f: e.tensor_mul(
                tmp[:, f * SLAB:(f + 1) * SLAB],
                srcbuf[:, f * SLAB:(f + 1) * SLAB], pb[7][:, 0:SLAB]),
                rd=[src_b, PB[7]], wr=[tmp_b])
            cb.v(lambda e, f=f: e.tensor_sub(
                tmp[:, f * SLAB:(f + 1) * SLAB],
                tmp[:, f * SLAB:(f + 1) * SLAB], pb[6][:, 0:SLAB]),
                rd=[tmp_b, PB[6]], wr=[tmp_b])

            def f3(e, f=f):
                b_b = ap3(bvec, bcol0 + f, [[bvw, 128], [0, SLAB]])
                return e.scalar_tensor_tensor(
                    dst[:, f * SLAB:(f + 1) * SLAB],
                    tmp[:, f * SLAB:(f + 1) * SLAB],
                    vcol(gcol + f), b_b, OP.mult, OP.add)
            cb.v(f3, rd=[tmp_b, B["vecs"], bvec_b], wr=[dst_b])

    def quant_tm(src, src_b, sw, dst, dst_b, dw, col0, F,
                 s1, s1b, s1w, c1, s2, s2b, s2w, c2, s3, s3b, s3w, c3):
        NB = F // 16
        sA = ap3(src, col0, [[sw, 128], [16, NB], [1, 16]])
        cb.v(lambda e: e.tensor_reduce(ub[:, 0:NB], sA, AX.X, OP.max,
                                       apply_absolute_value=True),
             rd=[src_b], wr=[B["ub"]])
        cb.v(lambda e: e.tensor_scalar(ub[:, 256:256 + NB], ub[:, 0:NB],
                                       r6c[:, 0:1], 1e-12, OP.mult, OP.max),
             rd=[B["ub"], B["r6c"]], wr=[B["ub"]])
        cb.v(lambda e: e.reciprocal(ub[:, 0:NB], ub[:, 256:256 + NB]),
             rd=[B["ub"]], wr=[B["ub"]])
        vf = ap3(s1, c1, [[s1w, 128], [1, F]])
        v3 = ap3(s1, c1, [[s1w, 128], [16, NB], [1, 16]])
        cb.a(lambda e: e.activation(vf, ap3(src, col0, [[sw, 128], [1, F]]),
                                    AF.Abs), rd=[src_b], wr=[s1b])
        rsb = ap3(ub, 0, [[SLAB, 128], [1, NB], [0, 16]])
        cb.v(lambda e: e.tensor_tensor(v3, v3, rsb, OP.mult),
             rd=[s1b, B["ub"]], wr=[s1b])
        tf = ap3(s2, c2, [[s2w, 128], [1, F]])
        kf = ap3(s3, c3, [[s3w, 128], [1, F]])
        cb.v(lambda e: e.tensor_single_scalar(tf, vf, 0.25, OP.is_gt),
             rd=[s1b], wr=[s2b])
        for thr in (0.75, 1.25, 1.75):
            cb.v(lambda e, thr=thr: e.scalar_tensor_tensor(
                tf, vf, thr, tf, OP.is_gt, OP.add), rd=[s1b, s2b], wr=[s2b])
        cb.v(lambda e: e.tensor_single_scalar(kf, vf, 2.5, OP.is_gt),
             rd=[s1b], wr=[s3b])
        for thr in (3.5, 5.0, 5.0):
            cb.v(lambda e, thr=thr: e.scalar_tensor_tensor(
                kf, vf, thr, kf, OP.is_gt, OP.add), rd=[s1b, s3b], wr=[s3b])
        cb.v(lambda e: e.scalar_tensor_tensor(tf, tf, 0.5, kf, OP.mult,
                                              OP.add),
             rd=[s2b, s3b], wr=[s2b])
        cb.a(lambda e: e.activation(vf, ap3(src, col0, [[sw, 128], [1, F]]),
                                    AF.Sign), rd=[src_b], wr=[s1b])
        cb.v(lambda e: e.tensor_tensor(tf, tf, vf, OP.mult),
             rd=[s2b, s1b], wr=[s2b])
        t3 = ap3(s2, c2, [[s2w, 128], [16, NB], [1, 16]])
        d3 = ap3(dst, col0, [[dw, 128], [16, NB], [1, 16]])
        ssb = ap3(ub, 256, [[SLAB, 128], [1, NB], [0, 16]])
        cb.v(lambda e: e.tensor_tensor(d3, t3, ssb, OP.mult),
             rd=[s2b, B["ub"]], wr=[dst_b])

    for s in range(NS):
        def ldx(e, s=s):
            o = ap3(xb, 0, [[SL, 128], [D, NCT], [1, D]])
            i = ap3(xin, s * SLAB * D, [[D, 128], [128 * D, NCT], [1, D]])
            return e.dma_start(out=o, in_=i)
        cb.dml(ldx, wr=[B["xb"]])
        ln_stats_tm(xb, B["xb"], tb, B["tb"], m1b, B["m1b"], r1b, B["r1b"],
                    s * NCT, EPS_LN)
        ln_apply_tm(xb, B["xb"], tb, B["tb"], m1b, B["m1b"], r1b, B["r1b"],
                    s * NCT, 32)
        tr_tm_to_fm(tb, B["tb"], hhi, B["hhi"], hlo, B["hlo"], (1, 2),
                    affine=(M_CA1, M_SA), aff_tmp=kb, aff_tmp_b=B["kb"])
        cb.dms(lambda e, s=s: e.dma_start(out=h_hi_d.ap()[s], in_=hhi[:, :]),
               rd=[B["hhi"]], wr=[SP["h"]])
        cb.dms(lambda e, s=s: e.dma_start(out=h_lo_d.ap()[s], in_=hlo[:, :]),
               rd=[B["hlo"]], wr=[SP["h"]])
        for f in range(KC):
            bank = 3 + f % 2
            mm3_fm(bank, wa_hi, wa_lo, (B["wa_hi"], B["wa_lo"]),
                   hhi, hlo, (B["hhi"], B["hlo"]), f, 0, 1024)
            cb.a(lambda e, f=f, bank=bank: e.activation(
                ub[:, :], pb[bank][:, 0:SLAB], AF.Identity,
                bias=vcol(C_BQ + f)), rd=[PB[bank], B["vecs"]], wr=[B["ub"]])
            cb.v(lambda e: e.tensor_scalar_min(t1s[:, :], ub[:, :], 0.0),
                 rd=[B["ub"]], wr=[B["t1s"]])
            cb.a(lambda e: e.activation(t2s[:, :], t1s[:, :], AF.Exp),
                 rd=[B["t1s"]], wr=[B["t2s"]])
            cb.v(lambda e, f=f: e.scalar_tensor_tensor(
                qb[:, f * SLAB:(f + 1) * SLAB], ub[:, :], 0.0, t2s[:, :],
                OP.max, OP.add), rd=[B["ub"], B["t2s"]], wr=[B["qb"]])
        cb.dms(lambda e, s=s: e.dma_start(out=q_d.ap()[s], in_=qb[:, :]),
               rd=[B["qb"]], wr=[SP["q"]])

    if stop_after == "a1":
        return _finish_queues(cb, nc)
    load_split(wk_in, D, 0, 0, D, 1024, wa_hi, wa_lo, B["wa_hi"], B["wa_lo"])
    load_split(wv_in, D, 0, 0, D, 1024, wb_hi, wb_lo, B["wb_hi"], B["wb_lo"])
    for s in range(NS):
        cb.dml(lambda e, s=s: e.dma_start(out=hhi[:, :], in_=h_hi_d.ap()[s]),
               rd=[SP["h"]], wr=[B["hhi"]])
        cb.dml(lambda e, s=s: e.dma_start(out=hlo[:, :], in_=h_lo_d.ap()[s]),
               rd=[SP["h"]], wr=[B["hlo"]])
        for proj in ("k", "v"):
            whi_, wlo_ = (wa_hi, wa_lo) if proj == "k" else (wb_hi, wb_lo)
            whib = B["wa_hi"] if proj == "k" else B["wb_hi"]
            wlob = B["wa_lo"] if proj == "k" else B["wb_lo"]
            for c in range(NCT):
                for nh in range(2):
                    bank = 3 + (c * 2 + nh) % 2
                    pbk = pb[bank]

                    def mm(e, c=c, nh=nh, pbk=pbk, whi_=whi_, wlo_=wlo_):
                        r = None
                        n = 0
                        for (wsrc, xsrc) in ((whi_, hhi), (wlo_, hhi),
                                             (whi_, hlo)):
                            for kc in range(KC):
                                r = e.matmul(
                                    pbk[:, 0:512],
                                    xsrc[:, kc * SLAB + c * 128:
                                         kc * SLAB + (c + 1) * 128],
                                    wsrc[:, kc * 1024 + nh * 512:
                                         kc * 1024 + (nh + 1) * 512],
                                    start=(n == 0), stop=(n == 3 * KC - 1))
                                n += 1
                        return r
                    cb.t(mm, rd=[whib, wlob, B["hhi"], B["hlo"]],
                         wr=[PB[bank]])
                    if proj == "k":
                        cb.v(lambda e, nh=nh, pbk=pbk: e.tensor_add(
                            ub[:, :], pbk[:, 0:512],
                            rep[:, R_BK + nh * 512:R_BK + (nh + 1) * 512]),
                            rd=[PB[bank], B["rep"]], wr=[B["ub"]])
                        cb.v(lambda e: e.tensor_scalar_min(
                            t1s[:, :], ub[:, :], 0.0),
                            rd=[B["ub"]], wr=[B["t1s"]])
                        cb.a(lambda e: e.activation(t2s[:, :], t1s[:, :],
                                                    AF.Exp),
                             rd=[B["t1s"]], wr=[B["t2s"]])
                        cb.v(lambda e, c=c, nh=nh: e.scalar_tensor_tensor(
                            kb[:, c * D + nh * 512:c * D + (nh + 1) * 512],
                            ub[:, :], 0.0, t2s[:, :], OP.max, OP.add),
                            rd=[B["ub"], B["t2s"]], wr=[B["kb"]])
                    else:
                        cb.v(lambda e, c=c, nh=nh, pbk=pbk: e.tensor_add(
                            tb[:, c * D + nh * 512:c * D + (nh + 1) * 512],
                            pbk[:, 0:512],
                            rep[:, R_BV + nh * 512:R_BV + (nh + 1) * 512]),
                            rd=[PB[bank], B["rep"]], wr=[B["tb"]])
        for c in range(NCT):
            bank = 5 + c % 2
            pbk = pb[bank]
            p7off = (c % 2) * 65

            def mmkv(e, c=c, pbk=pbk, p7off=p7off):
                r = None
                for hh in range(H):
                    hp, half = hh // 2, hh % 2
                    dst = (pbk[half * 64:(half + 1) * 64,
                               hp * 65:hp * 65 + 64] if hp < 7 else
                           pb[7][half * 64:(half + 1) * 64,
                                 p7off:p7off + 64])
                    r = e.matmul(
                        dst,
                        kb[:, c * D + hh * 64:c * D + (hh + 1) * 64],
                        tb[:, c * D + hh * 64:c * D + (hh + 1) * 64],
                        start=True, stop=True)
                for hh in range(H):
                    hp, half = hh // 2, hh % 2
                    dst = (pbk[half * 64:(half + 1) * 64,
                               hp * 65 + 64:hp * 65 + 65] if hp < 7 else
                           pb[7][half * 64:(half + 1) * 64,
                                 p7off + 64:p7off + 65])
                    r = e.matmul(
                        dst,
                        kb[:, c * D + hh * 64:c * D + (hh + 1) * 64],
                        onescol[:, :],
                        start=True, stop=True)
                return r
            cb.t(mmkv, rd=[B["kb"], B["tb"], B["onescol"]],
                 wr=[PB[bank], PB[7]])
            cb.v(lambda e, pbk=pbk: e.tensor_add(
                qb[:, 0:455], qb[:, 0:455], pbk[:, 0:455]),
                rd=[PB[bank], B["qb"]], wr=[B["qb"]])
            cb.v(lambda e, p7off=p7off: e.tensor_add(
                qb[:, 455:520], qb[:, 455:520],
                pb[7][:, p7off:p7off + 65]),
                rd=[PB[7], B["qb"]], wr=[B["qb"]])

    if stop_after == "a2":
        return _finish_queues(cb, nc)
    cb.dms(lambda e: e.dma_start(out=cc_in.ap()[:, :], in_=qb[:, 0:520]),
           rd=[B["qb"]], wr=[Bcc])

    if skip_collective:
        cb.dms(lambda e: e.dma_start(out=cc_out.ap()[:, :],
                                     in_=cc_in.ap()[:, :]),
               rd=[Bcc], wr=[Bcc])
    else:
        def ccf(e):
            return e.collective_compute(
                "AllReduce", OP.add,
                replica_groups=[[0, 1, 2, 3], [4, 5, 6, 7]],
                ins=[cc_in.ap().opt()], outs=[cc_out.ap().opt()])
        cb._emit("gpsimd", ccf, [Bcc], [Bcc], 1, False, semkey="cc")
    cb.dml(lambda e: e.dma_start(out=qb[:, 0:520], in_=cc_out.ap()[:, :]),
           rd=[Bcc], wr=[B["qb"]])

    def expand_kv(half):
        def f(e):
            o = ap3(kvexp, half * (64 * 8 * 130 + 65),
                    [[8 * 130, 64], [130, 8], [1, 65]])
            i = ap3(qb, half * 64 * SL, [[SL, 64], [65, 8], [1, 65]])
            return e.dma_start(out=o, in_=i)
        cb.dms(f, rd=[B["qb"]], wr=[B["kvexp"]])
    expand_kv(0)
    expand_kv(1)

    for j, rc in enumerate((R_GA, R_GB, R_CB1, R_SB)):
        replicate(rc, garow_d, j * D)

    if stop_after == "cc":
        return _finish_queues(cb, nc)
    load_split(wo_in, D, 0, 0, D, 1024, wa_hi, wa_lo, B["wa_hi"], B["wa_lo"])
    x1hi = wb_hi
    x1lo = wb_lo
    for s in range(NS):
        cb.dml(lambda e, s=s: e.dma_start(out=qb[:, :], in_=q_d.ap()[s]),
               rd=[SP["q"]], wr=[B["qb"]])
        cb.dml(lambda e, s=s: e.dma_start(out=hhi[:, :], in_=h_hi_d.ap()[s]),
               rd=[SP["h"]], wr=[B["hhi"]])
        cb.dml(lambda e, s=s: e.dma_start(out=hlo[:, :], in_=h_lo_d.ap()[s]),
               rd=[SP["h"]], wr=[B["hlo"]])
        for c in range(NCT):
            def mmo(e, c=c):
                r = None
                for hp in range(8):
                    b_i = hp // 3
                    off = (hp % 3) * 130
                    r = e.matmul(
                        pb[b_i][:, off:off + 130],
                        qb[:, hp * SLAB + c * 128:hp * SLAB + (c + 1) * 128],
                        kvexp[:, hp * 130:(hp + 1) * 130],
                        start=True, stop=True)
                return r
            cb.t(mmo, rd=[B["qb"], B["kvexp"]], wr=[PB[0], PB[1], PB[2]])
            for b_i, nh in ((0, 6), (1, 6), (2, 4)):
                cb.v(lambda e, b_i=b_i, nh=nh: e.tensor_scalar_add(
                    zb[:, b_i * 6:b_i * 6 + nh],
                    ap3(pb[b_i], 64, [[512, 128], [65, nh]]), EPS_Z),
                    rd=[PB[b_i]], wr=[B["zb"]])
            cb.v(lambda e: e.reciprocal(zb[:, 32:48], zb[:, 0:16]),
                 rd=[B["zb"]], wr=[B["zb"]])
            for b_i, nh in ((0, 6), (1, 6), (2, 4)):
                def f(e, c=c, b_i=b_i, nh=nh):
                    o = ap3(xb, c * D + b_i * 6 * 64,
                            [[SL, 128], [64, nh], [1, 64]])
                    i = ap3(pb[b_i], 0, [[512, 128], [65, nh], [1, 64]])
                    i2 = ap3(zb, 32 + b_i * 6, [[64, 128], [1, nh], [0, 64]])
                    return e.tensor_tensor(o, i, i2, OP.mult)
                cb.v(f, rd=[PB[b_i], B["zb"]], wr=[B["xb"]])
        tr_tm_to_fm(xb, B["xb"], wb_hi, B["wb_hi"], wb_lo, B["wb_lo"],
                    (3, 4), dcol0=4096)
        for f in range(KC):
            bank = 5 + f % 2
            mm3_fm(bank, wa_hi, wa_lo, (B["wa_hi"], B["wa_lo"]),
                   wb_hi, wb_lo, (B["wb_hi"], B["wb_lo"]), f, 0, 1024,
                   rcol0=4096)
            cb.a(lambda e, f=f, bank=bank: e.activation(
                ub[:, :], pb[bank][:, 0:SLAB], AF.Identity,
                bias=vcol(C_BO + f)), rd=[PB[bank], B["vecs"]], wr=[B["ub"]])
            cb.v(lambda e, f=f: e.tensor_add(
                t1s[:, :], ub[:, :], hhi[:, f * SLAB:(f + 1) * SLAB]),
                rd=[B["ub"], B["hhi"]], wr=[B["t1s"]])
            cb.v(lambda e, f=f: e.tensor_add(
                a2b[:, f * SLAB:(f + 1) * SLAB], t1s[:, :],
                hlo[:, f * SLAB:(f + 1) * SLAB]),
                rd=[B["t1s"], B["hlo"]], wr=[B["a2b"]])
        fm_ln(a2b, B["a2b"], C_G1, vecs, B["vecs"], 80, C_BE1,
              x1b, B["x1b"], EPS_LN, tb, B["tb"])
        cb.a(lambda e: e.activation(x1hi[:, 0:SL], x1b[:, :], AF.Identity),
             rd=[B["x1b"]], wr=[B["wb_hi"]])
        cb.v(lambda e: e.tensor_sub(x1lo[:, 0:SL], x1b[:, :],
                                    x1hi[:, 0:SL]),
             rd=[B["x1b"], B["wb_hi"]], wr=[B["wb_lo"]])
        cb.dms(lambda e, s=s: e.dma_start(out=x1_hi_d.ap()[s],
                                          in_=x1hi[:, 0:SL]),
               rd=[B["wb_hi"]], wr=[SP["x1"]])
        cb.dms(lambda e, s=s: e.dma_start(out=x1_lo_d.ap()[s],
                                          in_=x1lo[:, 0:SL]),
               rd=[B["wb_lo"]], wr=[SP["x1"]])

    if stop_after == "bc1":
        return _finish_queues(cb, nc)
    for st in range(bc2_stripes):
        load_split(w1_in, FF, 0, st * 512, D, 512, wa_hi, wa_lo,
                   B["wa_hi"], B["wa_lo"], dcol0=0)
        load_split(w2_in, D, st * 512, 0, 512, 1024, wa_hi, wa_lo,
                   B["wa_hi"], B["wa_lo"], dcol0=4096)
        for s in range(NS):
            cb.dml(lambda e, s=s: e.dma_start(out=hhi[:, :],
                                              in_=x1_hi_d.ap()[s]),
                   rd=[SP["x1"]], wr=[B["hhi"]])
            cb.dml(lambda e, s=s: e.dma_start(out=hlo[:, :],
                                              in_=x1_lo_d.ap()[s]),
                   rd=[SP["x1"]], wr=[B["hlo"]])
            if st > 0:
                ysrc = ya_d if st % 2 == 1 else yb_d
                ysb = SP["ya" if st % 2 == 1 else "yb"]
                cb.dml(lambda e, s=s, ysrc=ysrc: e.dma_start(
                    out=a2b[:, :], in_=ysrc.ap()[s]), rd=[ysb], wr=[B["a2b"]])
            for gf in range(4):
                bank = gf % 2
                mm3_fm(bank, wa_hi, wa_lo, (B["wa_hi"], B["wa_lo"]),
                       hhi, hlo, (B["hhi"], B["hlo"]), gf, 0, 512)
                cb.a(lambda e, gf=gf, bank=bank, st=st: e.activation(
                    tb[:, gf * SLAB:(gf + 1) * SLAB], pb[bank][:, 0:SLAB],
                    AF.Gelu, bias=vcol(C_B1 + st * 4 + gf)),
                    rd=[PB[bank], B["vecs"]], wr=[B["tb"]])
                cb.a(lambda e, gf=gf: e.activation(
                    wb_hi[:, gf * SLAB:(gf + 1) * SLAB],
                    tb[:, gf * SLAB:(gf + 1) * SLAB], AF.Identity),
                    rd=[B["tb"]], wr=[B["wb_hi"]])
                cb.v(lambda e, gf=gf: e.tensor_sub(
                    wb_lo[:, gf * SLAB:(gf + 1) * SLAB],
                    tb[:, gf * SLAB:(gf + 1) * SLAB],
                    wb_hi[:, gf * SLAB:(gf + 1) * SLAB]),
                    rd=[B["tb"], B["wb_hi"]], wr=[B["wb_lo"]])
            for yo in range(KC):
                bank = 2 + yo % 2
                mm3_fm(bank, wa_hi, wa_lo, (B["wa_hi"], B["wa_lo"]),
                       wb_hi, wb_lo, (B["wb_hi"], B["wb_lo"]), yo, 4096,
                       1024, nkc=4)
                if st == 0:
                    cb.a(lambda e, yo=yo, bank=bank: e.activation(
                        qb[:, yo * SLAB:(yo + 1) * SLAB],
                        pb[bank][:, 0:SLAB], AF.Identity,
                        bias=vcol(C_B2 + yo)),
                        rd=[PB[bank], B["vecs"]], wr=[B["qb"]])
                else:
                    cb.v(lambda e, yo=yo, bank=bank: e.tensor_add(
                        qb[:, yo * SLAB:(yo + 1) * SLAB],
                        pb[bank][:, 0:SLAB],
                        a2b[:, yo * SLAB:(yo + 1) * SLAB]),
                        rd=[PB[bank], B["a2b"]], wr=[B["qb"]])
            ydst = ya_d if st % 2 == 0 else yb_d
            yds = SP["ya" if st % 2 == 0 else "yb"]
            cb.dms(lambda e, s=s, ydst=ydst: e.dma_start(
                out=ydst.ap()[s], in_=qb[:, :]), rd=[B["qb"]], wr=[yds])

    if stop_after == "bc2":
        return _finish_queues(cb, nc)
    yfin_d = yb_d if bc2_stripes % 2 == 0 else ya_d
    yfin_sp = SP["yb" if bc2_stripes % 2 == 0 else "ya"]
    for s in range(NS):
        cb.dml(lambda e, s=s: e.dma_start(out=qb[:, :], in_=yfin_d.ap()[s]),
               rd=[yfin_sp], wr=[B["qb"]])
        cb.dml(lambda e, s=s: e.dma_start(out=hhi[:, :], in_=x1_hi_d.ap()[s]),
               rd=[SP["x1"]], wr=[B["hhi"]])
        cb.dml(lambda e, s=s: e.dma_start(out=hlo[:, :], in_=x1_lo_d.ap()[s]),
               rd=[SP["x1"]], wr=[B["hlo"]])

        def ldx(e, s=s):
            o = ap3(xb, 0, [[SL, 128], [D, NCT], [1, D]])
            i = ap3(xin, s * SLAB * D, [[D, 128], [128 * D, NCT], [1, D]])
            return e.dma_start(out=o, in_=i)
        cb.dml(ldx, wr=[B["xb"]])
        cb.v(lambda e: e.tensor_add(a2b[:, :], qb[:, :], hhi[:, :]),
             rd=[B["qb"], B["hhi"]], wr=[B["a2b"]])
        cb.v(lambda e: e.tensor_add(a2b[:, :], a2b[:, :], hlo[:, :]),
             rd=[B["a2b"], B["hlo"]], wr=[B["a2b"]])
        fm_ln(a2b, B["a2b"], C_G2, be2s, B["be2s"], 8, 0, x1b, B["x1b"],
              EPS_LN, tb, B["tb"])
        for c in range(NCT):
            for half in range(2):
                bank = 3 + half
                pbk = pb[bank]

                def tr(e, c=c, half=half, pbk=pbk):
                    r = None
                    for j in range(4):
                        f = half * 4 + j
                        r = e.transpose(
                            pbk[:, j * 128:(j + 1) * 128],
                            x1b[:, f * SLAB + c * 128:
                                f * SLAB + (c + 1) * 128],
                            ident[:, :])
                    return r
                cb.t(tr, rd=[B["x1b"], B["ident_s"]], wr=[PB[bank]])
                cb.v(lambda e, c=c, half=half, pbk=pbk: e.tensor_mul(
                    tb[:, c * D + half * 512:c * D + (half + 1) * 512],
                    pbk[:, 0:512],
                    rep[:, R_GA + half * 512:R_GA + (half + 1) * 512]),
                    rd=[PB[bank], B["rep"]], wr=[B["tb"]])
            cb.v(lambda e, c=c: e.tensor_add(
                kb[:, c * D:(c + 1) * D], tb[:, c * D:(c + 1) * D],
                xb[:, c * D:(c + 1) * D]),
                rd=[B["tb"], B["xb"]], wr=[B["kb"]])
        cb.dms(lambda e, s=s: e.dma_start(out=x2_d.ap()[s], in_=kb[:, :]),
               rd=[B["kb"]], wr=[SP["x2"]])
        ln_stats_tm(kb, B["kb"], tb, B["tb"], m4b, B["m4b"], r4b, B["r4b"],
                    0, EPS_LN)
        ln_apply_tm(kb, B["kb"], tb, B["tb"], m4b, B["m4b"], r4b, B["r4b"],
                    0, 4)
        for c in range(NCT):
            cb.v(lambda e, c=c: e.tensor_mul(
                a2b[:, c * D:(c + 1) * D], tb[:, c * D:(c + 1) * D],
                rep[:, R_CB1:R_CB1 + D]),
                rd=[B["tb"], B["rep"]], wr=[B["a2b"]])
            cb.v(lambda e, c=c: e.tensor_add(
                a2b[:, c * D:(c + 1) * D], a2b[:, c * D:(c + 1) * D],
                rep[:, R_SB:R_SB + D]),
                rd=[B["a2b"], B["rep"]], wr=[B["a2b"]])
        quant_tm(a2b, B["a2b"], SL, a2b, B["a2b"], SL, 0, SL,
                 qb, B["qb"], SL, 0, tb, B["tb"], SL, 0, kb, B["kb"], SL, 0)
        tr_tm_to_fm(a2b, B["a2b"], wb_hi, B["wb_hi"], wb_lo, B["wb_lo"],
                    (1, 2), dcol0=0)
        cb.dms(lambda e, s=s: e.dma_start(out=xq_hi_d.ap()[s],
                                          in_=wb_hi[:, 0:SL]),
               rd=[B["wb_hi"]], wr=[SP["xq"]])
        cb.dms(lambda e, s=s: e.dma_start(out=xq_lo_d.ap()[s],
                                          in_=wb_lo[:, 0:SL]),
               rd=[B["wb_lo"]], wr=[SP["xq"]])

    if stop_after == "bc3":
        return _finish_queues(cb, nc)
    STRIPES = [(i, min(512, FI - i)) for i in range(0, FI, 512)]

    def prep_fp4_weight(w_dram, wcols, i0, isz, along_rows, dst_hi, dst_lo,
                        dsthib, dstlob, dcol0):
        if along_rows:
            nrc, ncols = isz // 128, wcols
            off = i0 * wcols
        else:
            nrc, ncols = D // 128, isz
            off = i0

        def ld(e):
            o = ap3(xb, 0, [[SL, 128], [ncols, nrc], [1, ncols]])
            i = ap3(w_dram, off, [[wcols, 128], [128 * wcols, nrc],
                                  [1, ncols]])
            return e.dma_start(out=o, in_=i)
        cb.dml(ld, wr=[B["xb"]])
        total = nrc * ncols
        quant_tm(xb, B["xb"], SL, xb, B["xb"], SL, 0, total,
                 qb, B["qb"], SL, 0, tb, B["tb"], SL, 0, kb, B["kb"], SL, 0)
        nkc = ncols // 128
        for kc in range(nkc):
            for grp in range((nrc + 3) // 4):
                bank = 5 + grp % 2
                pbk = pb[bank]
                cnt = min(4, nrc - grp * 4)

                def tr(e, kc=kc, grp=grp, cnt=cnt, pbk=pbk):
                    r = None
                    for j in range(cnt):
                        rc = grp * 4 + j
                        r = e.transpose(
                            pbk[:, j * 128:(j + 1) * 128],
                            xb[:, rc * ncols + kc * 128:
                               rc * ncols + (kc + 1) * 128],
                            ident[:, :])
                    return r
                cb.t(tr, rd=[B["xb"], B["ident_s"]], wr=[PB[bank]])
                dcol = dcol0 + kc * (nrc * 128) + grp * 512
                cb.a(lambda e, dcol=dcol, cnt=cnt, pbk=pbk: e.activation(
                    dst_hi[:, dcol:dcol + cnt * 128], pbk[:, 0:cnt * 128],
                    AF.Identity), rd=[PB[bank]], wr=[dsthib])
                cb.v(lambda e, dcol=dcol, cnt=cnt, pbk=pbk: e.tensor_sub(
                    dst_lo[:, dcol:dcol + cnt * 128], pbk[:, 0:cnt * 128],
                    dst_hi[:, dcol:dcol + cnt * 128]),
                    rd=[PB[bank], dsthib], wr=[dstlob])

    for sti, (i0, isz) in enumerate(STRIPES):
        nic = isz // 128
        prep_fp4_weight(wg_in, D, i0, isz, True, wa_hi, wa_lo,
                        B["wa_hi"], B["wa_lo"], 0)
        prep_fp4_weight(wu_in, D, i0, isz, True, wa_hi, wa_lo,
                        B["wa_hi"], B["wa_lo"], 4096)
        prep_fp4_weight(wd_in, FI, i0, isz, False, wb_hi, wb_lo,
                        B["wb_hi"], B["wb_lo"], 0)
        last = sti == len(STRIPES) - 1
        for s in range(NS):
            cb.dml(lambda e, s=s: e.dma_start(out=hhi[:, :],
                                              in_=xq_hi_d.ap()[s]),
                   rd=[SP["xq"]], wr=[B["hhi"]])
            cb.dml(lambda e, s=s: e.dma_start(out=hlo[:, :],
                                              in_=xq_lo_d.ap()[s]),
                   rd=[SP["xq"]], wr=[B["hlo"]])
            if sti > 0:
                ysrc = y2a_d if sti % 2 == 1 else y2b_d
                ysb = SP["y2a" if sti % 2 == 1 else "y2b"]
                cb.dml(lambda e, s=s, ysrc=ysrc: e.dma_start(
                    out=qb[:, :], in_=ysrc.ap()[s]), rd=[ysb], wr=[B["qb"]])
            if last:
                cb.dml(lambda e, s=s: e.dma_start(out=xb[:, :],
                                                  in_=x2_d.ap()[s]),
                       rd=[SP["x2"]], wr=[B["xb"]])
            for c in range(NCT):
                for bank, wc0 in ((0, 0), (1, 4096)):
                    def mm(e, c=c, bank=bank, wc0=wc0, isz=isz):
                        r = None
                        n = 0
                        for (wsrc, xsrc) in ((wa_hi, hhi), (wa_lo, hhi),
                                             (wa_hi, hlo)):
                            for kc in range(KC):
                                r = e.matmul(
                                    pb[bank][:, 0:isz],
                                    xsrc[:, kc * SLAB + c * 128:
                                         kc * SLAB + (c + 1) * 128],
                                    wsrc[:, wc0 + kc * isz:
                                         wc0 + (kc + 1) * isz],
                                    start=(n == 0), stop=(n == 3 * KC - 1))
                                n += 1
                        return r
                    cb.t(mm, rd=[B["wa_hi"], B["wa_lo"], B["hhi"],
                                 B["hlo"]], wr=[PB[bank]])
                cb.a(lambda e, isz=isz: e.activation(
                    ub[:, 0:isz], pb[0][:, 0:isz], AF.Silu),
                    rd=[PB[0]], wr=[B["ub"]])
                cb.v(lambda e, isz=isz: e.tensor_mul(
                    a2b[:, 0:isz], ub[:, 0:isz], pb[1][:, 0:isz]),
                    rd=[B["ub"], PB[1]], wr=[B["a2b"]])
                quant_tm(a2b, B["a2b"], SL, a2b, B["a2b"], SL, 0, isz,
                         tb, B["tb"], SL, 0, tb, B["tb"], SL, 1024,
                         tb, B["tb"], SL, 2048)
                pbk = pb[2]

                def trh(e, nic=nic, pbk=pbk):
                    r = None
                    for ic in range(nic):
                        r = e.transpose(pbk[:, ic * 128:(ic + 1) * 128],
                                        a2b[:, ic * 128:(ic + 1) * 128],
                                        ident[:, :])
                    return r
                cb.t(trh, rd=[B["a2b"], B["ident_s"]], wr=[PB[2]])
                cb.a(lambda e, nic=nic, pbk=pbk: e.activation(
                    hq_hi[:, 0:nic * 128], pbk[:, 0:nic * 128], AF.Identity),
                    rd=[PB[2]], wr=[B["hq_hi"]])
                cb.v(lambda e, nic=nic, pbk=pbk: e.tensor_sub(
                    hq_lo[:, 0:nic * 128], pbk[:, 0:nic * 128],
                    hq_hi[:, 0:nic * 128]),
                    rd=[PB[2], B["hq_hi"]], wr=[B["hq_lo"]])
                for no in range(2):
                    bank = 3 + no

                    def mmd(e, c=c, no=no, bank=bank, nic=nic):
                        r = None
                        n = 0
                        for (wsrc, xsrc) in ((wb_hi, hq_hi), (wb_lo, hq_hi),
                                             (wb_hi, hq_lo)):
                            for ic in range(nic):
                                r = e.matmul(
                                    pb[bank][:, 0:512],
                                    xsrc[:, ic * 128:(ic + 1) * 128],
                                    wsrc[:, ic * 1024 + no * 512:
                                         ic * 1024 + (no + 1) * 512],
                                    start=(n == 0), stop=(n == 3 * nic - 1))
                                n += 1
                        return r
                    cb.t(mmd, rd=[B["wb_hi"], B["wb_lo"], B["hq_hi"],
                                  B["hq_lo"]], wr=[PB[bank]])
                    if sti == 0:
                        cb.v(lambda e, c=c, no=no, bank=bank: e.tensor_copy(
                            x1b[:, c * D + no * 512:c * D + (no + 1) * 512],
                            pb[bank][:, 0:512]),
                            rd=[PB[bank]], wr=[B["x1b"]])
                    elif not last:
                        cb.v(lambda e, c=c, no=no, bank=bank: e.tensor_add(
                            x1b[:, c * D + no * 512:c * D + (no + 1) * 512],
                            pb[bank][:, 0:512],
                            qb[:, c * D + no * 512:c * D + (no + 1) * 512]),
                            rd=[PB[bank], B["qb"]], wr=[B["x1b"]])
                    else:
                        cb.v(lambda e, c=c, no=no, bank=bank: e.tensor_add(
                            t1s[:, :], pb[bank][:, 0:512],
                            qb[:, c * D + no * 512:c * D + (no + 1) * 512]),
                            rd=[PB[bank], B["qb"]], wr=[B["t1s"]])
                        cb.v(lambda e, no=no: e.tensor_mul(
                            t2s[:, :], t1s[:, :],
                            rep[:, R_GB + no * 512:R_GB + (no + 1) * 512]),
                            rd=[B["t1s"], B["rep"]], wr=[B["t2s"]])
                        cb.v(lambda e, c=c, no=no: e.tensor_add(
                            x1b[:, c * D + no * 512:c * D + (no + 1) * 512],
                            t2s[:, :],
                            xb[:, c * D + no * 512:c * D + (no + 1) * 512]),
                            rd=[B["t2s"], B["xb"]], wr=[B["x1b"]])
            if not last:
                ydst = y2a_d if sti % 2 == 0 else y2b_d
                yds = SP["y2a" if sti % 2 == 0 else "y2b"]
                cb.dms(lambda e, s=s, ydst=ydst: e.dma_start(
                    out=ydst.ap()[s], in_=x1b[:, :]), rd=[B["x1b"]], wr=[yds])
            else:
                def stout(e, s=s):
                    oo = ap3(out, s * SLAB * D,
                             [[D, 128], [128 * D, NCT], [1, D]])
                    i = ap3(x1b, 0, [[SL, 128], [D, NCT], [1, D]])
                    return e.dma_start(out=oo, in_=i)
                cb.dms(stout, rd=[B["x1b"]], wr=[SP["out"]])

    return _finish_queues(cb, nc)


def _finish_queues(cb, nc):
    fin = [(cb.sem[f"gpsimd_d{j}"], cb.cnt[f"gpsimd_d{j}"])
           for j in range(NPOOL) if cb.cnt[f"gpsimd_d{j}"]]

    def _final(e, fin=fin):
        for s, v in fin:
            e.wait_ge(s, v)
    cb.q["gpsimd"].append(_final)

    from contextlib import ExitStack
    with ExitStack() as ctx:
        block = ctx.enter_context(nc.Block())

        @block.sync
        def _(sync):
            for g in cb.q["sync"]:
                g(sync)

        @block.scalar
        def _(scalar):
            for g in cb.q["scalar"]:
                g(scalar)

        @block.vector
        def _(vector):
            for g in cb.q["vector"]:
                g(vector)

        @block.tensor
        def _(tensor):
            for g in cb.q["tensor"]:
                g(tensor)

        @block.gpsimd
        def _(gpsimd):
            for g in cb.q["gpsimd"]:
                g(gpsimd)

    return nc


def make_in_maps(inputs):
    x = np.ascontiguousarray(np.asarray(inputs["L0_attn_src"],
                                        dtype=np.float32))
    c = np.asarray(inputs["L0_c"], dtype=np.float32)
    ident = np.eye(128, dtype=np.float32)

    def fm(v, k):
        return np.ascontiguousarray(
            np.asarray(v, dtype=np.float32).reshape(k, 128).T)

    def cc(v):
        return np.ascontiguousarray(np.asarray(v, dtype=np.float32))

    common = {
        "wmod": cc(inputs["L0_wmod"]),
        "wq": cc(inputs["L0_attn_Wq"]), "wk": cc(inputs["L0_attn_Wk"]),
        "wv": cc(inputs["L0_attn_Wv"]), "wo": cc(inputs["L0_attn_Wo"]),
        "w1": cc(inputs["L0_attn_W1"]), "w2": cc(inputs["L0_attn_W2"]),
        "wg": cc(inputs["L0_ffn_w_gate"]), "wu": cc(inputs["L0_ffn_w_up"]),
        "wd": cc(inputs["L0_ffn_w_down"]),
        "bq_fm": fm(inputs["L0_attn_bq"], 8),
        "bk_row": cc(np.asarray(inputs["L0_attn_bk"]).reshape(1, D)),
        "bv_row": cc(np.asarray(inputs["L0_attn_bv"]).reshape(1, D)),
        "bo_fm": fm(inputs["L0_attn_bo"], 8),
        "b1_fm": fm(inputs["L0_attn_b1"], 32),
        "b2_fm": fm(inputs["L0_attn_b2"], 8),
        "g1_fm": fm(inputs["L0_attn_g1"], 8),
        "be1_fm": fm(inputs["L0_attn_be1"], 8),
        "g2_fm": fm(inputs["L0_attn_g2"], 8),
        "be2_fm": fm(inputs["L0_attn_be2"], 8),
        "ident": ident,
    }
    in_maps = []
    for core in range(NCORES):
        b = core // 4
        t0 = (core % 4) * T
        m = dict(common)
        m["xin"] = np.ascontiguousarray(x[b, t0:t0 + T])
        m["cfm"] = fm(c[b], 8)
        in_maps.append(m)
    return in_maps


def kernel(**inputs):
    nc = build_program()
    in_maps = make_in_maps(inputs)
    res = run_bass_kernel_spmd(nc, in_maps, core_ids=list(range(NCORES)))
    full = np.empty((2, 16384, D), dtype=np.float32)
    for core in range(NCORES):
        b = core // 4
        t0 = (core % 4) * T
        full[b, t0:t0 + T] = res.results[core]["out"]
    return full
